# version 16
# baseline (speedup 1.0000x reference)
import numpy as np
import ml_dtypes
from contextlib import ExitStack

import concourse.bass as bass
import concourse.mybir as mybir
from concourse.bass_utils import run_bass_kernel_spmd

F32 = mybir.dt.float32
BF16 = mybir.dt.bfloat16
AF = mybir.ActivationFunctionType
OP = mybir.AluOpType
AX = mybir.AxisListType

D_MODEL = 1024
NMB = 8
E = 8
FEXP = 512
NFB = 4
FSH = 1024
NKS = 8
TAPS = 20
HTAP = 10
HALO = 32
LC = 1024
SUB = 512
NTC = LC // 128
TPS = SUB // 128
NUNIT = E + 2


def build(T):
    NCH = T // LC
    NSUB = 2 * NCH

    nc = bass.Bass("TRN2", target_bir_lowering=False)

    x_d = nc.dram_tensor("x", [T, D_MODEL], F32, kind="ExternalInput")
    alog_d = nc.dram_tensor("alog", [128, 128], F32, kind="ExternalInput")
    bmat_d = nc.dram_tensor("bmat", [128, 128], F32, kind="ExternalInput")
    ct_d = nc.dram_tensor("ct", [128, 128], F32, kind="ExternalInput")
    dpt_d = nc.dram_tensor("dpt", [128, NMB], F32, kind="ExternalInput")
    wr_d = nc.dram_tensor("wr", [D_MODEL, E], F32, kind="ExternalInput")
    wg_d = nc.dram_tensor("wg", [E, D_MODEL, FEXP], F32, kind="ExternalInput")
    wu_d = nc.dram_tensor("wu", [E, D_MODEL, FEXP], F32, kind="ExternalInput")
    wd_d = nc.dram_tensor("wd", [E, FEXP, D_MODEL], F32, kind="ExternalInput")
    sg_d = nc.dram_tensor("sgm", [D_MODEL, FSH], F32, kind="ExternalInput")
    su_d = nc.dram_tensor("sux", [D_MODEL, FSH], F32, kind="ExternalInput")
    sd_d = nc.dram_tensor("sdm", [FSH, D_MODEL], F32, kind="ExternalInput")
    eb_d = nc.dram_tensor("ebias", [128, NTC * E], F32, kind="ExternalInput")
    id_d = nc.dram_tensor("ident", [128, 128], F32, kind="ExternalInput")
    idb_d = nc.dram_tensor("identb", [128, 128], BF16, kind="ExternalInput")
    out_d = nc.dram_tensor("out", [T, D_MODEL], F32, kind="ExternalOutput")
    dgdram = nc.dram_tensor("dgdram", [NMB, 2, 128, HTAP, 128], BF16)
    wgbf = nc.dram_tensor("wgbf", [E, D_MODEL, FEXP], BF16)
    wubf = nc.dram_tensor("wubf", [E, D_MODEL, FEXP], BF16)
    wdbf = nc.dram_tensor("wdbf", [E, FEXP, D_MODEL], BF16)
    rwtd = nc.dram_tensor("rwtd", [2, NTC * E, 128], BF16)

    es = ExitStack()

    def sb(name, shape, dt=F32):
        return es.enter_context(nc.sbuf_tensor(name, shape, dt))

    i_sb = sb("i_sb", [128, 128])
    eb_sb = sb("eb_sb", [128, NTC * E])
    wr_sb = sb("wr_sb", [128, NKS, E], BF16)
    sg_sb = sb("sg_sb", [128, NKS, FSH], BF16)
    su_sb = sb("su_sb", [128, NKS, FSH], BF16)
    sd_sb = sb("sd_sb", [128, NKS, D_MODEL], BF16)
    wg_sb = sb("wg_sb", [128, 2, NKS, FEXP], BF16)
    wu_sb = sb("wu_sb", [128, 2, NKS, FEXP], BF16)
    wd_sb = sb("wd_sb", [128, 2, NFB, D_MODEL], BF16)
    dg_sb = sb("dg_sb", [128, 3, HTAP, 128], BF16)
    xnT = sb("xnT", [128, NMB, HALO + LC], BF16)
    xln = sb("xln", [128, 3, D_MODEL])
    stats = sb("stats", [128, 12])
    aggr = sb("aggr", [128, 2 * NTC])
    varep = sb("varep", [128, NTC])
    rstd = sb("rstd", [128, NTC])
    tns = sb("tns", [128, NTC])
    ktmp = ExitStack()

    def kt(name, shape, dt=F32):
        return ktmp.enter_context(nc.sbuf_tensor(name, shape, dt))

    alog2 = kt("alog2", [128, 128])
    bmat2 = kt("bmat2", [128, 128])
    ct2 = kt("ct2", [128, 128])
    w2p = kt("w2p", [128, 128])
    w2n = kt("w2n", [128, 128])
    pws = kt("pws", [128, 2, 128])
    pwb = kt("pwb", [128, 128])
    ib_sb = kt("ib_sb", [128, 128], BF16)
    kdt = kt("kdt", [128, TAPS, NMB])
    dpt_sb = kt("dpt_sb", [128, NMB])
    dgG = kt("dgG", [128, HTAP, 128], BF16)
    ktmp.close()

    rwb = sb("rwb", [128, 2, SUB], BF16)
    lg64 = sb("lg64", [128, NTC * E])
    m1 = sb("m1", [128, NTC])
    m2 = sb("m2", [128, NTC])
    eq1 = sb("eq1", [128, NTC * E])
    eq2 = sb("eq2", [128, NTC * E])
    msk = sb("msk", [128, NTC * E])
    e2v = sb("e2v", [128, NTC])
    e2x = sb("e2x", [128, NTC])
    den = sb("den", [128, NTC])
    w1v = sb("w1v", [128, NTC])
    w2v = sb("w2v", [128, NTC])
    rwf = sb("rwf", [128, NTC * E])
    rwTT = sb("rwTT", [64, 128], BF16)
    ctmp = sb("ctmp", [128, D_MODEL], BF16)
    h_sb = sb("h_sb", [128, E, NFB, SUB], BF16)
    hsh = sb("hsh", [128, NKS, SUB], BF16)
    stg = sb("stg", [128, 2, SUB])
    sT_sb = sb("sT_sb", [128, TPS, D_MODEL], BF16)
    outb = sb("outb", [128, 2, D_MODEL])
    gev = sb("gev", [128, 2, SUB], BF16)
    gcm2 = sb("gcm2", [128, 2, D_MODEL], BF16)

    P = [es.enter_context(nc.psum_tensor(f"P{i}", [128, 512], F32)) for i in range(8)]

    sems = {}
    for name in [
        "vch", "sch",
        "d_xl0", "d_xl1", "d_xl2", "d_xp0", "d_xp1", "d_xp2", "d_rww", "d_rwb0", "d_rwb1",
        "d_xr0", "d_xr1", "d_st0", "d_st1",
        "d_w0", "d_wsg", "d_wsu", "d_wsd", "d_dgw", "d_wg0", "d_wg1", "d_wu0", "d_wu1",
        "d_wd0", "d_wd1", "d_dg0", "d_dg1", "d_dg2", "d_sq", "g_ms",
        "d_pcg0", "d_pcg1", "d_pcu0", "d_pcu1", "d_pcd0", "d_pcd1",
        "a_kexp", "v_kb", "v_dg",
        "v_lnstat", "a_rstd", "v_ln", "t_tp", "v_evac",
        "t_rt", "v_rtr1", "a_e2", "v_rw", "t_rwtp", "v_rwT",
        "a_s", "t_dgfree",
        "t_gate", "a_gev", "t_up", "v_h", "t_wdfree",
        "t_down", "a_g", "v_c", "t_sT", "a_sT", "v_out",
    ]:
        sems[name] = es.enter_context(nc.semaphore(name))
    S = sems

    def xl_cnt(cc, i):
        s = i % 3
        per = 3 if s < 2 else 2
        return 16 * (cc * per + i // 3 + 1)

    def xp_cnt(cc, i):
        s = i % 3
        per = 3 if s < 2 else 2
        return 16 * (cc * per + i // 3 + 1)

    def slot2_cnt(m):
        return 16 * (m // 2 + 1)

    def xr_cnt(u, q):
        return 16 * (2 * u + q // 2 + 1)

    SQ_ALL = 16 * 7

    def g_of(c, i):
        return c * NTC + i

    def u_of(c, j):
        return c * 2 + j

    def m3(ap3):
        return ap3.rearrange("p a b -> p (a b)")

    def wslot_wait(eng, sem, m):
        if m >= 2:
            m2 = m - 2
            u2, uu2 = divmod(m2, 8)
            eng.wait_ge(S[sem], 40 * u2 + 4 * uu2 + 4)

    with es:
        with nc.Block() as blk:

            @blk.sync
            def _(sy):
                def prepass_load(cc, i):
                    s = i % 3
                    if cc >= 1:
                        last = {0: 6, 1: 7, 2: 5}[s]
                        sy.wait_ge(S["t_tp"], 8 * (NTC * (cc - 1) + last + 1))
                    if i >= 3:
                        sy.wait_ge(S["v_lnstat"], NTC * cc + (i - 3) + 1)
                    sy.dma_start(out=xln[:, s, :],
                                 in_=x_d[128 * (NTC * cc + i):128 * (NTC * cc + i + 1), :]
                                 ).then_inc(S["d_xp" + str(s)], 16)

                def xln_reload(c2, i):
                    g = g_of(c2, i)
                    s = i % 3
                    if i < 3:
                        last = {0: 6, 1: 7, 2: 5}[s]
                        sy.wait_ge(S["v_lnstat"], NTC * c2 + last + 1)
                    else:
                        sy.wait_ge(S["t_tp"], 8 * (g - 2))
                    sy.dma_start(out=xln[:, s, :],
                                 in_=x_d[128 * g:128 * (g + 1), :]
                                 ).then_inc(S["d_xl" + str(s)], 16)

                def dg_load_sy(u, hs):
                    m = 16 * u + hs
                    if u == 0:
                        sy.wait_ge(S["d_dgw"], 16 * (hs + 1))
                    if m >= 3:
                        sy.wait_ge(S["t_dgfree"], m - 2)
                    mb2, half2 = divmod(hs, 2)
                    sy.dma_start(out=dg_sb[:, m % 3], in_=dgdram[mb2, half2]
                                 ).then_inc(S["d_dg" + str(m % 3)], 16)

                def wd_item_sy(u, e):
                    m = 8 * u + e
                    if m >= 2:
                        sy.wait_ge(S["t_wdfree"], m - 1)
                    if m < 16:
                        sy.wait_ge(S["d_pcd" + str(m % 2)], 16 * 4)
                    sy.dma_start(out=wd_sb[:, m % 2],
                                 in_=wdbf[e].rearrange("(k p) f -> p k f", p=128)
                                 ).then_inc(S["d_wd" + str(m % 2)], 16)

                for i in range(NTC):
                    prepass_load(0, i)
                for i in range(3):
                    xln_reload(0, i)
                for i in range(3, NTC):
                    xln_reload(0, i)
                for c in range(NCH):
                    for hs in range(1, 16, 2):
                        dg_load_sy(u_of(c, 0), hs)
                    if c >= 2:
                        kmax = 8 * u_of(c - 2, 1) + E - 1
                        for sl in range(2):
                            n = (kmax - sl) // 2 + 1 if kmax >= sl else 0
                            if n > 0:
                                sy.wait_ge(S["d_rwb" + str(sl)], 16 * n)
                    sy.wait_ge(S["v_rwT"], c + 1)
                    sy.dma_start(out=rwtd[c % 2], in_=rwTT[:, :]).then_inc(S["d_rww"], 16)
                    for j in range(2):
                        u = u_of(c, j)
                        if u >= 1:
                            wd_item_sy(u, 0)
                            wd_item_sy(u, 1)
                        for e in range(E):
                            k = 8 * u + e
                            if j == 0 and e == 0:
                                sy.wait_ge(S["d_rww"], 16 * (c + 1))
                            if k >= 2:
                                pu, pe = divmod(k - 2, 8)
                                sy.wait_ge(S["v_h"], 40 * pu + 4 * (pe + 1))
                            ap = bass.AP(rwtd, ((c % 2) * NTC * E + 4 * j * E + e) * 128,
                                         [[0, 128], [E * 128, TPS], [1, 128]])
                            sy.dma_start(out=rwb[:, k % 2, :],
                                         in_=ap).then_inc(S["d_rwb" + str(k % 2)], 16)
                        if u >= 1:
                            for e in range(2, E):
                                wd_item_sy(u, e)
                        if j == 0 and c + 1 < NCH:
                            for i in range(NTC):
                                prepass_load(c + 1, i)
                            for i in range(3):
                                xln_reload(c + 1, i)
                        if j == 0:
                            for hs in range(1, 16, 2):
                                dg_load_sy(u_of(c, 1), hs)
                    if c + 1 < NCH:
                        for i in range(3, NTC):
                            xln_reload(c + 1, i)
                sy.wait_ge(S["d_st0"], 16 * 2 * 2 * NCH)
                sy.wait_ge(S["d_st1"], 16 * 2 * 2 * NCH)

            @blk.gpsimd
            def _(gp):
                gp.memset(xnT[:, :, 0:HALO], 0.0).then_inc(S["g_ms"], 1)
                gp.dma_start(out=wr_sb[:, :, :],
                             in_=wr_d.rearrange("(k p) f -> p k f", p=128)
                             ).then_inc(S["d_w0"], 16)

                PC = {"wg": (wg_d, wgbf, wg_sb, "d_wg", "d_pcg", "t_gate"),
                      "wu": (wu_d, wubf, wu_sb, "d_wu", "d_pcu", "t_up"),
                      "wd": (wd_d, wdbf, wd_sb, "d_wd", "d_pcd", "t_wdfree")}

                def ls(kind, e):
                    srcd, dstd, slot, csem, psem, gsem = PC[kind]
                    if e >= 2:
                        if kind == "wd":
                            gp.wait_ge(S["t_wdfree"], e - 1)
                        else:
                            wslot_wait(gp, gsem, e)
                        gp.wait_ge(S[psem + str(e % 2)], 16 * ((e - 2) // 2 + 1))
                    gp.dma_start(out=slot[:, e % 2],
                                 in_=srcd[e].rearrange("(k p) f -> p k f", p=128)
                                 ).then_inc(S[csem + str(e % 2)], 16)
                    gp.wait_ge(S[csem + str(e % 2)], slot2_cnt(e))
                    gp.dma_start(out=dstd[e].rearrange("(k p) f -> p k f", p=128),
                                 in_=slot[:, e % 2]).then_inc(S[psem + str(e % 2)], 16)

                def stream_item(kind, u, e):
                    srcd, dstd, slot, csem, psem, gsem = PC[kind]
                    m = 8 * u + e
                    wslot_wait(gp, gsem, m)
                    if m < 10:
                        gp.wait_ge(S[psem + str(m % 2)], 16 * 4)
                    gp.dma_start(out=slot[:, m % 2],
                                 in_=dstd[e].rearrange("(k p) f -> p k f", p=128)
                                 ).then_inc(S[csem + str(m % 2)], 16)

                def xres_st(u):
                    order = [("xres", 0), ("xres", 1), ("st", 0), ("xres", 2),
                             ("st", 1), ("xres", 3), ("st", 2), ("st", 3)]
                    for kind, q in order:
                        row = 128 * (u * TPS + q)
                        if kind == "xres":
                            if 4 * u + q >= 2:
                                gp.wait_ge(S["d_st" + str(q % 2)],
                                           xr_cnt(u, q) - 16)
                            gp.dma_start(out=outb[:, q % 2, :],
                                         in_=x_d[row:row + 128, :]
                                         ).then_inc(S["d_xr" + str(q % 2)], 16)
                        else:
                            gp.wait_ge(S["v_out"], 4 * u + q + 1)
                            gp.dma_start(out=out_d[row:row + 128, :],
                                         in_=outb[:, q % 2, :]
                                         ).then_inc(S["d_st" + str(q % 2)], 16)

                ls("wg", 0); ls("wu", 0); ls("wg", 1); ls("wu", 1)
                for e in range(2, E):
                    ls("wg", e)
                    ls("wu", e)
                    if e == 2:
                        gp.dma_start(out=sg_sb[:, :, :],
                                     in_=sg_d.rearrange("(k p) f -> p k f", p=128)
                                     ).then_inc(S["d_wsg"], 16)
                    if e == 3:
                        gp.dma_start(out=su_sb[:, :, :],
                                     in_=su_d.rearrange("(k p) f -> p k f", p=128)
                                     ).then_inc(S["d_wsu"], 16)
                    if e == 4:
                        gp.dma_start(out=sd_sb[:, :, :],
                                     in_=sd_d.rearrange("(k p) f -> p k f", p=128)
                                     ).then_inc(S["d_wsd"], 16)
                ls("wd", 0); ls("wd", 1)
                for e in range(2, E):
                    ls("wd", e)
                xres_st(0)
                for u in range(1, NSUB):
                    for e in range(E):
                        stream_item("wg", u, e)
                        stream_item("wu", u, e)
                    xres_st(u)

            @blk.vector
            def _(v):
                vn = [0]

                def ch(inst):
                    vn[0] += 1
                    inst.then_inc(S["vch"], 1)
                    v.wait_ge(S["vch"], vn[0])
                    return inst

                def mark(sem, n=1):
                    v.sem_inc(S[sem], n)

                def ln_prepass(cc):
                    for i in range(NTC):
                        v.wait_ge(S["d_xp" + str(i % 3)], xp_cnt(cc, i))
                        sl = xln[:, i % 3, :]
                        ch(v.bn_stats(stats[:, 0:6], sl[:, 0:512]))
                        ch(v.bn_stats(stats[:, 6:12], sl[:, 512:1024]))
                        ch(v.bn_aggr(aggr[:, 2 * i:2 * i + 2], stats[:, :]))
                        ch(v.tensor_scalar(varep[:, i:i + 1], aggr[:, 2 * i + 1:2 * i + 2],
                                           1e-5, None, OP.add))
                        mark("v_lnstat")

                ln_prepass(0)
                v.wait_ge(S["d_sq"], SQ_ALL)
                ch(v.tensor_mul(w2p[:, :], ct2[:, :], bmat2[:, :]))
                ch(v.tensor_scalar(w2n[:, :], w2p[:, :], -1.0, None, OP.mult))

                def ln_norm(cc, i):
                    g = g_of(cc, i)
                    v.wait_ge(S["a_rstd"], g + 1)
                    v.wait_ge(S["d_xl" + str(i % 3)], xl_cnt(cc, i))
                    if i >= 3:
                        v.wait_ge(S["t_tp"], 8 * (g - 2))
                    ch(v.tensor_scalar(xln[:, i % 3, :], xln[:, i % 3, :],
                                       aggr[:, 2 * i:2 * i + 1], rstd[:, i:i + 1],
                                       OP.subtract, OP.mult))
                    mark("v_ln")

                for c in range(NCH):
                    if c == 0:
                        for i in range(NTC):
                            ln_norm(c, i)

                    if c == 0:
                        for d in range(TAPS):
                            v.wait_ge(S["a_kexp"], d + 1)
                            sgn = w2p if d % 2 == 0 else w2n
                            ch(v.tensor_mul(pwb[:, :], pws[:, d % 2, :], sgn[:, :]))
                            mark("v_kb")
                            ch(v.reduce_sum(kdt[:, d, :],
                                            pwb[:, :].rearrange("p (a b) -> p a b", b=16),
                                            axis=AX.X))
                        ch(v.tensor_add(kdt[:, 0, :], kdt[:, 0, :], dpt_sb[:, :]))

                    def b3(t64):
                        return t64[:].rearrange("p (i e) -> p i e", e=E)

                    def bc8(t8):
                        return bass.AP(t8, 0, [[NTC, 128], [1, NTC], [0, E]])

                    v.wait_ge(S["t_rt"], NTC * (c + 1))
                    v.wait_ge(S["d_sq"], SQ_ALL)
                    ch(v.tensor_add(lg64[:, :], P[2][:, 0:NTC * E], eb_sb[:, :]))
                    ch(v.reduce_max(m1[:, :], b3(lg64), axis=AX.X))
                    ch(v.tensor_tensor(b3(eq1), b3(lg64), bc8(m1), OP.is_equal))
                    ch(v.tensor_scalar(msk[:, :], eq1[:, :], 1e9, None, OP.mult))
                    ch(v.tensor_sub(msk[:, :], lg64[:, :], msk[:, :]))
                    ch(v.reduce_max(m2[:, :], b3(msk), axis=AX.X))
                    ch(v.tensor_tensor(b3(eq2), b3(msk), bc8(m2), OP.is_equal))
                    ch(v.tensor_sub(e2v[:, :], m2[:, :], m1[:, :]))
                    mark("v_rtr1")
                    v.wait_ge(S["a_e2"], c + 1)
                    ch(v.tensor_scalar(den[:, :], e2x[:, :], 1.0, None, OP.add))
                    ch(v.reciprocal(w1v[:, :], den[:, :]))
                    ch(v.tensor_mul(w2v[:, :], e2x[:, :], w1v[:, :]))
                    ch(v.tensor_tensor(b3(rwf), b3(eq1), bc8(w1v), OP.mult))
                    ch(v.tensor_tensor(b3(eq2), b3(eq2), bc8(w2v), OP.mult))
                    ch(v.tensor_add(rwf[:, :], rwf[:, :], eq2[:, :]))
                    mark("v_rw")

                    if c == 0:
                        ib_ap = bass.AP(ib_sb, 0, [[128, 128], [0, HTAP], [1, 128]])
                        for k in range(16):
                            mb2, half2 = divmod(k, 2)
                            if k >= 1:
                                v.wait_ge(S["d_dgw"], 16 * k)
                            kap = bass.AP(kdt, (half2 * HTAP) * NMB + mb2,
                                          [[TAPS * NMB, 128], [NMB, HTAP], [0, 128]])
                            ch(v.tensor_tensor(dgG[:, :, :], ib_ap, kap, OP.mult))
                            mark("v_dg")

                    v.wait_ge(S["t_rwtp"], c + 1)
                    if c >= 1:
                        v.wait_ge(S["d_rww"], 16 * c)
                    ch(v.tensor_copy(rwTT[:, :], P[3][0:64, 0:128]))
                    mark("v_rwT")

                    for j in range(2):
                        u = u_of(c, j)
                        for uu in range(NUNIT):
                            for fb in range(NFB):
                                kk = 40 * u + 4 * uu + fb
                                if uu == 0 and fb == 0 and u >= 1:
                                    v.wait_ge(S["t_down"], 8 * u)
                                v.wait_ge(S["a_gev"], kk + 1)
                                if uu < E:
                                    if fb == 0:
                                        kk8 = 8 * u + uu
                                        v.wait_ge(S["d_rwb" + str(kk8 % 2)],
                                                  16 * (kk8 // 2 + 1))
                                    ch(v.tensor_mul(gev[:, kk % 2, :], gev[:, kk % 2, :],
                                                    rwb[:, (8 * u + uu) % 2, :]))
                                src0 = gev[:, kk % 2, :]
                                v.wait_ge(S["t_up"], kk + 1)
                                if uu < E:
                                    dst = h_sb[:, uu, fb, :]
                                else:
                                    dst = hsh[:, (uu - E) * 4 + fb, :]
                                ch(v.tensor_mul(dst, src0, P[fb][:, 0:512]))
                                mark("v_h")
                        if j == 1 and c + 1 < NCH:
                            v.wait_ge(S["t_dgfree"], 32 * c + 16)
                            v.wait_ge(S["v_evac"], 8 * c + 8)
                            ch(v.tensor_copy(xnT[:, :, 0:HALO], xnT[:, :, LC:LC + HALO]))
                            for i in range(3):
                                ln_norm(c + 1, i)
                        if j == 0 and c + 1 < NCH:
                            ln_prepass(c + 1)
                        for q in range(TPS):
                            v.wait_ge(S["a_g"], 8 * u + 2 * q + 2)
                            v.wait_ge(S["a_sT"], 8 * (u + 1))
                            ch(v.tensor_mul(ctmp[:, :], sT_sb[:, q, :],
                                            gcm2[:, q % 2, :]))
                            mark("v_c")
                            v.wait_ge(S["d_xr" + str(q % 2)], xr_cnt(u, q))
                            ch(v.tensor_add(outb[:, q % 2, :], outb[:, q % 2, :],
                                            ctmp[:, :]))
                            mark("v_out")
                            if j == 1 and c + 1 < NCH:
                                ln_norm(c + 1, 3 + q)
                        if j == 1 and c + 1 < NCH:
                            ln_norm(c + 1, 7)

            @blk.scalar
            def _(s):
                sn = [0]

                def ch(inst):
                    sn[0] += 1
                    inst.then_inc(S["sch"], 1)
                    s.wait_ge(S["sch"], sn[0])
                    return inst

                def mark(sem, n=1):
                    s.sem_inc(S[sem], n)

                s.dma_start(out=i_sb[:, :], in_=id_d[:, :]).then_inc(S["d_sq"], 16)
                s.dma_start(out=ib_sb[:, :], in_=idb_d[:, :]).then_inc(S["d_sq"], 16)
                s.dma_start(out=eb_sb[:, :], in_=eb_d[:, :]).then_inc(S["d_sq"], 16)
                s.dma_start(out=dpt_sb[:, :], in_=dpt_d[:, :]).then_inc(S["d_sq"], 16)
                s.dma_start(out=alog2[:, :], in_=alog_d[:, :]).then_inc(S["d_sq"], 16)
                s.dma_start(out=bmat2[:, :], in_=bmat_d[:, :]).then_inc(S["d_sq"], 16)
                s.dma_start(out=ct2[:, :], in_=ct_d[:, :]).then_inc(S["d_sq"], 16)
                def rstd_calc(cc):
                    for i in range(NTC):
                        g = g_of(cc, i)
                        s.wait_ge(S["v_lnstat"], g + 1)
                        ch(s.activation(tns[:, i:i + 1], varep[:, i:i + 1], AF.Ln))
                        ch(s.activation(rstd[:, i:i + 1], tns[:, i:i + 1],
                                        AF.Exp, scale=-0.5))
                        mark("a_rstd")

                def kexp(d):
                    s.wait_ge(S["d_sq"], SQ_ALL)
                    if d >= 2:
                        s.wait_ge(S["v_kb"], d - 1)
                    ch(s.activation(pws[:, d % 2, :], alog2[:, :], AF.Exp,
                                    scale=float(d)))
                    mark("a_kexp")

                rstd_calc(0)
                for d in range(2):
                    kexp(d)
                for c in range(NCH):
                    for i in range(NTC):
                        g = g_of(c, i)
                        s.wait_ge(S["t_tp"], 8 * (g + 1))
                        pp, pq = (P[4], P[5]) if i % 2 == 0 else (P[6], P[7])
                        col = HALO + i * 128
                        ch(s.activation(xnT[:, 0:4, col:col + 128],
                                        pp[:, 0:512].rearrange("p (a b) -> p a b", b=128),
                                        AF.Copy))
                        ch(s.activation(xnT[:, 4:8, col:col + 128],
                                        pq[:, 0:512].rearrange("p (a b) -> p a b", b=128),
                                        AF.Copy))
                        mark("v_evac")
                        if c == 0 and i == 3:
                            for d in range(2, 12):
                                kexp(d)
                        if c == 0 and i == 7:
                            for d in range(12, TAPS):
                                kexp(d)
                    s.wait_ge(S["v_rtr1"], c + 1)
                    ch(s.activation(e2x[:, :], e2v[:, :], AF.Exp))
                    mark("a_e2")
                    for j in range(2):
                        u = u_of(c, j)
                        if j == 1 and c + 1 < NCH:
                            rstd_calc(c + 1)

                        def dg_load(hs):
                            m = 16 * u + hs
                            if u == 0:
                                s.wait_ge(S["d_dgw"], 16 * (hs + 1))
                            if m >= 3:
                                s.wait_ge(S["t_dgfree"], m - 2)
                            mb2, half2 = divmod(hs, 2)
                            s.dma_start(out=dg_sb[:, m % 3], in_=dgdram[mb2, half2]
                                        ).then_inc(S["d_dg" + str(m % 3)], 16)

                        def dgw(k):
                            s.wait_ge(S["v_dg"], k + 1)
                            mb2, half2 = divmod(k, 2)
                            s.dma_start(out=dgdram[mb2, half2],
                                        in_=dgG[:, :, :]).then_inc(S["d_dgw"], 16)

                        if u == 0:
                            dgw(0)
                            dgw(1)
                            dg_load(0)
                            dgw(2)
                            dgw(3)
                            dg_load(2)
                        for mb in range(NMB):
                            if u == 0 and mb < 6:
                                dgw(2 * mb + 4)
                                dgw(2 * mb + 5)
                            s.wait_ge(S["t_dgfree"], 16 * u + 2 * mb + 2)
                            if mb >= 2:
                                s.wait_ge(S["t_sT"], 8 * u + mb - 1)
                            ch(s.activation(stg[:, mb % 2, :], P[mb % 2][:, 0:512], AF.Copy))
                            mark("a_s")
                            if mb < 6:
                                dg_load(2 * mb + 4)
                            if mb >= 1:
                                s.wait_ge(S["t_sT"], 8 * u + mb)
                                ch(s.activation(
                                    sT_sb[:, 0:TPS, (mb - 1) * 128:mb * 128],
                                    P[2 + (mb - 1) % 2][:, 0:512]
                                    .rearrange("p (a b) -> p a b", b=128),
                                    AF.Copy))
                                mark("a_sT")
                        s.wait_ge(S["t_sT"], 8 * u + 8)
                        ch(s.activation(
                            sT_sb[:, 0:TPS, (NMB - 1) * 128:NMB * 128],
                            P[2 + (NMB - 1) % 2][:, 0:512]
                            .rearrange("p (a b) -> p a b", b=128),
                            AF.Copy))
                        mark("a_sT")
                        for uu in range(NUNIT):
                            for fb in range(NFB):
                                kk = 40 * u + 4 * uu + fb
                                s.wait_ge(S["t_gate"], kk + 1)
                                if kk >= 2:
                                    s.wait_ge(S["v_h"], kk - 1)
                                ch(s.activation(gev[:, kk % 2, :], P[4 + fb][:, 0:512], AF.Silu))
                                mark("a_gev")
                        for b in range(NMB):
                            q, dh = divmod(b, 2)
                            kk = 8 * u + b
                            s.wait_ge(S["t_down"], kk + 1)
                            if q >= 2:
                                s.wait_ge(S["v_c"], 4 * u + q - 1)
                            ch(s.activation(gcm2[:, q % 2, dh * 512:(dh + 1) * 512],
                                            P[b][:, 0:512], AF.Silu))
                            mark("a_g")
                        if u + 1 < NSUB:
                            for hs2 in (0, 2):
                                m2 = 16 * (u + 1) + hs2
                                s.wait_ge(S["t_dgfree"], m2 - 2)
                                mb3, half3 = divmod(hs2, 2)
                                s.dma_start(out=dg_sb[:, m2 % 3], in_=dgdram[mb3, half3]
                                            ).then_inc(S["d_dg" + str(m2 % 3)], 16)

            @blk.tensor
            def _(t):
                def sT_tp(u, j, c, mb):
                    t.wait_ge(S["a_s"], 8 * u + mb + 1)
                    if mb >= 2:
                        t.wait_ge(S["a_sT"], 8 * u + mb - 1)
                    else:
                        if u >= 1:
                            t.wait_ge(S["a_sT"], 8 * (u - 1) + 7 + mb)
                        if mb == 0:
                            t.wait_ge(S["v_rtr1"], c + 1)
                        if mb == 1 and j == 1:
                            t.wait_ge(S["v_rwT"], c + 1)
                    ti = None
                    for q in range(TPS):
                        ti = t.transpose(P[2 + mb % 2][:, q * 128:(q + 1) * 128],
                                         stg[:, mb % 2, q * 128:(q + 1) * 128],
                                         i_sb[:, :])
                    ti.then_inc(S["t_sT"], 1)

                def ln_pair(i):
                    return (P[4], P[5]) if i % 2 == 0 else (P[6], P[7])

                def ln_tp(cc, i, bank_wait=True):
                    g = g_of(cc, i)
                    t.wait_ge(S["v_ln"], g + 1)
                    if bank_wait and i >= 2:
                        t.wait_ge(S["v_evac"], g - 1)
                    pp, pq = ln_pair(i)
                    for q in range(NMB):
                        dst = (pp if q < 4 else pq)[:, (q % 4) * 128:(q % 4 + 1) * 128]
                        t.transpose(dst, xln[:, i % 3, q * 128:(q + 1) * 128],
                                    i_sb[:, :]).then_inc(S["t_tp"], 1)

                for c in range(NCH):
                    for i in range(0 if c == 0 else 2, NTC):
                        ln_tp(c, i)
                    for i in range(NTC):
                        g = g_of(c, i)
                        t.wait_ge(S["v_evac"], g + 1)
                        if i == 0:
                            if c > 0:
                                t.wait_ge(S["a_sT"], 16 * c)
                                t.wait_ge(S["v_rtr1"], c)
                                t.wait_ge(S["v_rwT"], c)
                            else:
                                t.wait_ge(S["d_w0"], 16)
                        mi = None
                        for k in range(NKS):
                            mi = t.matmul(P[2][:, E * i:E * (i + 1)],
                                          xnT[:, k, HALO + i * 128:HALO + (i + 1) * 128],
                                          wr_sb[:, k, :], start=(k == 0), stop=(k == NKS - 1))
                        mi.then_inc(S["t_rt"], 1)
                    for j in range(2):
                        u = u_of(c, j)
                        t.wait_ge(S["v_evac"], NTC * c + TPS * (j + 1))
                        if u == 0:
                            t.wait_ge(S["g_ms"], 1)
                        else:
                            t.wait_ge(S["a_g"], 8 * (u - 1) + 2)
                        for mb in range(NMB):
                            mi = None
                            for half in range(2):
                                m = 16 * u + 2 * mb + half
                                t.wait_ge(S["d_dg" + str(m % 3)], 16 * (m // 3 + 1))
                                if mb >= 2 and half == 0:
                                    t.wait_ge(S["a_s"], 8 * u + mb - 1)
                                for dd in range(HTAP):
                                    d = half * HTAP + dd
                                    rhs = xnT[:, mb,
                                              HALO + j * SUB - d:HALO + j * SUB - d + SUB]
                                    mi = t.matmul(P[mb % 2][:, 0:512], dg_sb[:, m % 3, dd, :],
                                                  rhs, start=(d == 0), stop=(d == TAPS - 1))
                                mi.then_inc(S["t_dgfree"], 1)
                            if mb >= 1:
                                sT_tp(u, j, c, mb - 1)
                        sT_tp(u, j, c, NMB - 1)
                        if j == 0:
                            t.wait_ge(S["v_rw"], c + 1)
                            t.wait_ge(S["a_sT"], 8 * u + 8)
                            t.transpose(P[3][0:64, 0:128], rwf[:, :],
                                        i_sb[:, :]).then_inc(S["t_rwtp"], 1)
                        if u >= 1:
                            t.wait_ge(S["a_g"], 8 * u)
                        for uu in range(NUNIT):
                            m = 8 * u + uu
                            if uu < E:
                                t.wait_ge(S["d_wg" + str(m % 2)], slot2_cnt(m))
                            elif uu == E and u == 0:
                                t.wait_ge(S["d_wsg"], 16)
                            for fb in range(NFB):
                                kk = 40 * u + 4 * uu + fb
                                if kk >= 4:
                                    t.wait_ge(S["a_gev"], kk - 3)
                                else:
                                    t.wait_ge(S["v_evac"], NTC * c + (7 if fb < 2 else 8))
                                mi = None
                                for k in range(NKS):
                                    if uu < E:
                                        lhs = wg_sb[:, m % 2, k, fb * 128:(fb + 1) * 128]
                                    else:
                                        fsb = (uu - E) * 4 + fb
                                        lhs = sg_sb[:, k, fsb * 128:(fsb + 1) * 128]
                                    rhs = xnT[:, k, HALO + j * SUB:HALO + (j + 1) * SUB]
                                    mi = t.matmul(P[4 + fb][:, 0:512], lhs, rhs,
                                                  start=(k == 0), stop=(k == NKS - 1))
                                mi.then_inc(S["t_gate"], 1)
                            if uu < E:
                                t.wait_ge(S["d_wu" + str(m % 2)], slot2_cnt(m))
                            elif uu == E and u == 0:
                                t.wait_ge(S["d_wsu"], 16)
                            for fb in range(NFB):
                                kk = 40 * u + 4 * uu + fb
                                if kk >= 4:
                                    t.wait_ge(S["v_h"], kk - 3)
                                if uu == 0:
                                    if fb < 2:
                                        t.wait_ge(S["a_s"], 8 * u + 8)
                                    else:
                                        t.wait_ge(S["a_sT"], 8 * u + 5 + fb)
                                        t.wait_ge(S["v_rtr1"], c + 1)
                                        t.wait_ge(S["v_rwT"], c + 1)
                                mi = None
                                for k in range(NKS):
                                    if uu < E:
                                        lhs = wu_sb[:, m % 2, k, fb * 128:(fb + 1) * 128]
                                    else:
                                        fsb = (uu - E) * 4 + fb
                                        lhs = su_sb[:, k, fsb * 128:(fsb + 1) * 128]
                                    rhs = xnT[:, k, HALO + j * SUB:HALO + (j + 1) * SUB]
                                    mi = t.matmul(P[fb][:, 0:512], lhs, rhs,
                                                  start=(k == 0), stop=(k == NKS - 1))
                                mi.then_inc(S["t_up"], 1)

                        t.wait_ge(S["v_h"], 40 * (u + 1))
                        t.wait_ge(S["a_gev"], 40 * (u + 1))
                        for uu in range(NUNIT):
                            if uu < E:
                                m = 8 * u + uu
                                t.wait_ge(S["d_wd" + str(m % 2)], slot2_cnt(m))
                            elif uu == E and u == 0:
                                t.wait_ge(S["d_wsd"], 16)
                            for fs in range(NFB):
                                for q in range(TPS):
                                    if uu < E:
                                        lhs = h_sb[:, uu, fs, q * 128:(q + 1) * 128]
                                    else:
                                        fsb = (uu - E) * 4 + fs
                                        lhs = hsh[:, fsb, q * 128:(q + 1) * 128]
                                    for dh in range(2):
                                        if uu < E:
                                            rhs = wd_sb[:, m % 2, fs,
                                                        dh * 512:(dh + 1) * 512]
                                        else:
                                            fsb = (uu - E) * 4 + fs
                                            rhs = sd_sb[:, fsb, dh * 512:(dh + 1) * 512]
                                        mi = t.matmul(P[2 * q + dh][:, 0:512], lhs, rhs,
                                                      start=(uu == 0 and fs == 0),
                                                      stop=(uu == NUNIT - 1 and fs == NFB - 1))
                                        if uu == NUNIT - 1 and fs == NFB - 1:
                                            mi.then_inc(S["t_down"], 1)
                                        if (uu < E and fs == NFB - 1 and q == TPS - 1
                                                and dh == 1):
                                            mi.then_inc(S["t_wdfree"], 1)
                        if j == 1 and c + 1 < NCH:
                            t.wait_ge(S["a_g"], 8 * u + 6)
                            ln_tp(c + 1, 0, bank_wait=False)
                            t.wait_ge(S["a_g"], 8 * u + 8)
                            ln_tp(c + 1, 1, bank_wait=False)
    return nc


def _host_layout(inputs, T_full):
    f32 = np.float32

    def smix(a):
        return np.ascontiguousarray(
            np.asarray(a, dtype=f32).reshape(16, 8, 128).transpose(2, 1, 0)
            .reshape(128, 128))

    x = np.ascontiguousarray(inputs["L0_attn_x"], dtype=f32)
    dpt = np.ascontiguousarray(np.asarray(inputs["L0_attn_D"], dtype=f32)
                               .reshape(NMB, 128).T)
    eb = np.ascontiguousarray(
        np.broadcast_to(np.tile(np.asarray(inputs["L0_ffn_expert_bias"], dtype=f32), 8)
                        .reshape(1, 8 * E), (128, 8 * E)))
    common = {
        "alog": smix(inputs["L0_attn_A_log"]),
        "bmat": smix(inputs["L0_attn_B"]),
        "ct": smix(np.asarray(inputs["L0_attn_C"], dtype=f32).T),
        "dpt": dpt,
        "wr": np.ascontiguousarray(inputs["L0_ffn_Wr"], dtype=f32),
        "wg": np.ascontiguousarray(inputs["L0_ffn_Wg"], dtype=f32),
        "wu": np.ascontiguousarray(inputs["L0_ffn_Wu"], dtype=f32),
        "wd": np.ascontiguousarray(inputs["L0_ffn_Wd"], dtype=f32),
        "sgm": np.ascontiguousarray(inputs["L0_ffn_Sg"], dtype=f32),
        "sux": np.ascontiguousarray(inputs["L0_ffn_Su"], dtype=f32),
        "sdm": np.ascontiguousarray(inputs["L0_ffn_Sd"], dtype=f32),
        "ebias": eb,
        "ident": np.eye(128, dtype=f32),
        "identb": np.eye(128, dtype=ml_dtypes.bfloat16),
    }
    return x, common


def kernel(L0_attn_x, L0_attn_A_log, L0_attn_B, L0_attn_C, L0_attn_D,
           L0_ffn_Wr, L0_ffn_Wg, L0_ffn_Wu, L0_ffn_Wd,
           L0_ffn_Sg, L0_ffn_Su, L0_ffn_Sd, L0_ffn_expert_bias):
    inputs = dict(L0_attn_x=L0_attn_x, L0_attn_A_log=L0_attn_A_log,
                  L0_attn_B=L0_attn_B, L0_attn_C=L0_attn_C, L0_attn_D=L0_attn_D,
                  L0_ffn_Wr=L0_ffn_Wr, L0_ffn_Wg=L0_ffn_Wg, L0_ffn_Wu=L0_ffn_Wu,
                  L0_ffn_Wd=L0_ffn_Wd, L0_ffn_Sg=L0_ffn_Sg, L0_ffn_Su=L0_ffn_Su,
                  L0_ffn_Sd=L0_ffn_Sd, L0_ffn_expert_bias=L0_ffn_expert_bias)
    B, T, D = L0_attn_x.shape
    assert B == 8 and D == D_MODEL
    x, common = _host_layout(inputs, T)
    nc = build(T)
    in_maps = [dict(common, x=np.ascontiguousarray(x[b])) for b in range(B)]
    res = run_bass_kernel_spmd(nc, in_maps, core_ids=list(range(B)))
    out = np.stack([res.results[b]["out"] for b in range(B)], axis=0)
    return out.astype(np.float32)


# version 17
# speedup vs baseline: 1.1210x; 1.1210x over previous
import numpy as np
import ml_dtypes
from contextlib import ExitStack

import concourse.bass as bass
import concourse.mybir as mybir
from concourse.bass_utils import run_bass_kernel_spmd

F32 = mybir.dt.float32
BF16 = mybir.dt.bfloat16
AF = mybir.ActivationFunctionType
OP = mybir.AluOpType
AX = mybir.AxisListType

D_MODEL = 1024
NMB = 8
E = 8
FEXP = 512
NFB = 4
FSH = 1024
NKS = 8
TAPS = 20
HTAP = 10
HALO = 32
LC = 1024
SUB = 512
NTC = LC // 128
TPS = SUB // 128
NUNIT = E + 2


def build(T):
    NCH = T // LC
    NSUB = 2 * NCH

    nc = bass.Bass("TRN2", target_bir_lowering=False)

    x_d = nc.dram_tensor("x", [T, D_MODEL], F32, kind="ExternalInput")
    alog_d = nc.dram_tensor("alog", [128, 128], F32, kind="ExternalInput")
    bmat_d = nc.dram_tensor("bmat", [128, 128], F32, kind="ExternalInput")
    ct_d = nc.dram_tensor("ct", [128, 128], F32, kind="ExternalInput")
    dpt_d = nc.dram_tensor("dpt", [128, NMB], F32, kind="ExternalInput")
    wr_d = nc.dram_tensor("wr", [D_MODEL, E], F32, kind="ExternalInput")
    wg_d = nc.dram_tensor("wg", [E, D_MODEL, FEXP], F32, kind="ExternalInput")
    wu_d = nc.dram_tensor("wu", [E, D_MODEL, FEXP], F32, kind="ExternalInput")
    wd_d = nc.dram_tensor("wd", [E, FEXP, D_MODEL], F32, kind="ExternalInput")
    sg_d = nc.dram_tensor("sgm", [D_MODEL, FSH], F32, kind="ExternalInput")
    su_d = nc.dram_tensor("sux", [D_MODEL, FSH], F32, kind="ExternalInput")
    sd_d = nc.dram_tensor("sdm", [FSH, D_MODEL], F32, kind="ExternalInput")
    eb_d = nc.dram_tensor("ebias", [128, NTC * E], F32, kind="ExternalInput")
    id_d = nc.dram_tensor("ident", [128, 128], F32, kind="ExternalInput")
    idb_d = nc.dram_tensor("identb", [128, 128], BF16, kind="ExternalInput")
    out_d = nc.dram_tensor("out", [T, D_MODEL], F32, kind="ExternalOutput")
    dgdram = nc.dram_tensor("dgdram", [NMB, 2, 128, HTAP, 128], BF16)
    wgbf = nc.dram_tensor("wgbf", [E, D_MODEL, FEXP], BF16)
    wubf = nc.dram_tensor("wubf", [E, D_MODEL, FEXP], BF16)
    wdbf = nc.dram_tensor("wdbf", [E, FEXP, D_MODEL], BF16)
    rwtd = nc.dram_tensor("rwtd", [2, NTC * E, 128], BF16)

    es = ExitStack()

    def sb(name, shape, dt=F32):
        return es.enter_context(nc.sbuf_tensor(name, shape, dt))

    i_sb = sb("i_sb", [128, 128])
    eb_sb = sb("eb_sb", [128, NTC * E])
    wr_sb = sb("wr_sb", [128, NKS, E], BF16)
    sg_sb = sb("sg_sb", [128, NKS, FSH], BF16)
    su_sb = sb("su_sb", [128, NKS, FSH], BF16)
    sd_sb = sb("sd_sb", [128, NKS, D_MODEL], BF16)
    wg_sb = sb("wg_sb", [128, 2, NKS, FEXP], BF16)
    wu_sb = sb("wu_sb", [128, 2, NKS, FEXP], BF16)
    wd_sb = sb("wd_sb", [128, 2, NFB, D_MODEL], BF16)
    dg_sb = sb("dg_sb", [128, 4, HTAP, 128], BF16)
    xnT = sb("xnT", [128, NMB, HALO + LC], BF16)
    xln = sb("xln", [128, 3, D_MODEL])
    stats = sb("stats", [128, 12])
    aggr = sb("aggr", [128, 2 * NTC])
    varep = sb("varep", [128, NTC])
    rstd = sb("rstd", [128, NTC])
    tns = sb("tns", [128, NTC])
    ktmp = ExitStack()

    def kt(name, shape, dt=F32):
        return ktmp.enter_context(nc.sbuf_tensor(name, shape, dt))

    alog2 = kt("alog2", [128, 128])
    bmat2 = kt("bmat2", [128, 128])
    ct2 = kt("ct2", [128, 128])
    w2p = kt("w2p", [128, 128])
    w2n = kt("w2n", [128, 128])
    pws = kt("pws", [128, 2, 128])
    pwb = kt("pwb", [128, 128])
    ib_sb = kt("ib_sb", [128, 128], BF16)
    kdt = kt("kdt", [128, TAPS, NMB])
    dpt_sb = kt("dpt_sb", [128, NMB])
    dgG = kt("dgG", [128, HTAP, 128], BF16)
    ktmp.close()

    rwb = sb("rwb", [128, 2, SUB], BF16)
    lg64 = sb("lg64", [128, NTC * E])
    m1 = sb("m1", [128, NTC])
    m2 = sb("m2", [128, NTC])
    eq1 = sb("eq1", [128, NTC * E])
    eq2 = sb("eq2", [128, NTC * E])
    msk = sb("msk", [128, NTC * E])
    e2v = sb("e2v", [128, NTC])
    e2x = sb("e2x", [128, NTC])
    den = sb("den", [128, NTC])
    w1v = sb("w1v", [128, NTC])
    w2v = sb("w2v", [128, NTC])
    rwf = sb("rwf", [128, NTC * E])
    rwTT = sb("rwTT", [64, 128], BF16)
    ctmp = sb("ctmp", [128, D_MODEL], BF16)
    h_sb = sb("h_sb", [128, E, NFB, SUB], BF16)
    hsh = sb("hsh", [128, NKS, SUB], BF16)
    stg = sb("stg", [128, 2, SUB])
    sT_sb = sb("sT_sb", [128, TPS, D_MODEL], BF16)
    outb = sb("outb", [128, 2, D_MODEL])
    gev = sb("gev", [128, 2, SUB], BF16)
    gcm2 = sb("gcm2", [128, 2, D_MODEL], BF16)

    P = [es.enter_context(nc.psum_tensor(f"P{i}", [128, 512], F32)) for i in range(8)]

    sems = {}
    for name in [
        "vch", "sch",
        "d_xl0", "d_xl1", "d_xl2", "d_xp0", "d_xp1", "d_xp2", "d_rww", "d_rwb0", "d_rwb1",
        "d_xr0", "d_xr1", "d_st0", "d_st1",
        "d_w0", "d_wsg", "d_wsu", "d_wsd", "d_dgw", "d_wg0", "d_wg1", "d_wu0", "d_wu1",
        "d_wd0", "d_wd1", "d_dg0", "d_dg1", "d_dg2", "d_dg3", "d_sq", "g_ms",
        "d_pcg0", "d_pcg1", "d_pcu0", "d_pcu1", "d_pcd0", "d_pcd1",
        "a_kexp", "v_kb", "v_dg",
        "v_lnstat", "a_rstd", "v_ln", "t_tp", "v_evac",
        "t_rt", "v_rtr1", "a_e2", "v_rw", "t_rwtp", "v_rwT",
        "a_s", "t_dgfree",
        "t_gate", "a_gev", "t_up", "v_h", "t_wdfree",
        "t_down", "a_g", "v_c", "t_sT", "a_sT", "v_out",
    ]:
        sems[name] = es.enter_context(nc.semaphore(name))
    S = sems

    def xl_cnt(cc, i):
        s = i % 3
        per = 3 if s < 2 else 2
        return 16 * (cc * per + i // 3 + 1)

    def xp_cnt(cc, i):
        s = i % 3
        per = 3 if s < 2 else 2
        return 16 * (cc * per + i // 3 + 1)

    def slot2_cnt(m):
        return 16 * (m // 2 + 1)

    def xr_cnt(u, q):
        return 16 * (2 * u + q // 2 + 1)

    SQ_ALL = 16 * 7

    def g_of(c, i):
        return c * NTC + i

    def u_of(c, j):
        return c * 2 + j

    def m3(ap3):
        return ap3.rearrange("p a b -> p (a b)")

    def wslot_wait(eng, sem, m):
        if m >= 2:
            m2 = m - 2
            u2, uu2 = divmod(m2, 8)
            eng.wait_ge(S[sem], 40 * u2 + 4 * uu2 + 4)

    with es:
        with nc.Block() as blk:

            @blk.sync
            def _(sy):
                def prepass_load(cc, i):
                    s = i % 3
                    if cc >= 1:
                        last = {0: 6, 1: 7, 2: 5}[s]
                        sy.wait_ge(S["t_tp"], 8 * (NTC * (cc - 1) + last + 1))
                    if i >= 3:
                        sy.wait_ge(S["v_lnstat"], NTC * cc + (i - 3) + 1)
                    sy.dma_start(out=xln[:, s, :],
                                 in_=x_d[128 * (NTC * cc + i):128 * (NTC * cc + i + 1), :]
                                 ).then_inc(S["d_xp" + str(s)], 16)

                def xln_reload(c2, i):
                    g = g_of(c2, i)
                    s = i % 3
                    if i < 3:
                        last = {0: 6, 1: 7, 2: 5}[s]
                        sy.wait_ge(S["v_lnstat"], NTC * c2 + last + 1)
                    else:
                        sy.wait_ge(S["t_tp"], 8 * (g - 2))
                    sy.dma_start(out=xln[:, s, :],
                                 in_=x_d[128 * g:128 * (g + 1), :]
                                 ).then_inc(S["d_xl" + str(s)], 16)

                def dg_load_sy(u, hs):
                    m = 16 * u + hs
                    if u == 0:
                        sy.wait_ge(S["d_dgw"], 16 * (hs + 1))
                    if m >= 4:
                        sy.wait_ge(S["t_dgfree"], m - 3)
                    mb2, half2 = divmod(hs, 2)
                    sy.dma_start(out=dg_sb[:, m % 4], in_=dgdram[mb2, half2]
                                 ).then_inc(S["d_dg" + str(m % 4)], 16)

                def wd_item_sy(u, e):
                    m = 8 * u + e
                    if m >= 2:
                        sy.wait_ge(S["t_wdfree"], m - 1)
                    if m < 16:
                        sy.wait_ge(S["d_pcd" + str(m % 2)], 16 * 4)
                    sy.dma_start(out=wd_sb[:, m % 2],
                                 in_=wdbf[e].rearrange("(k p) f -> p k f", p=128)
                                 ).then_inc(S["d_wd" + str(m % 2)], 16)

                for i in range(NTC):
                    prepass_load(0, i)
                for i in range(3):
                    xln_reload(0, i)
                for i in range(3, NTC):
                    xln_reload(0, i)
                for c in range(NCH):
                    for hs in range(1, 16, 2):
                        dg_load_sy(u_of(c, 0), hs)
                    if c >= 2:
                        kmax = 8 * u_of(c - 2, 1) + E - 1
                        for sl in range(2):
                            n = (kmax - sl) // 2 + 1 if kmax >= sl else 0
                            if n > 0:
                                sy.wait_ge(S["d_rwb" + str(sl)], 16 * n)
                    sy.wait_ge(S["v_rwT"], c + 1)
                    sy.dma_start(out=rwtd[c % 2], in_=rwTT[:, :]).then_inc(S["d_rww"], 16)
                    for j in range(2):
                        u = u_of(c, j)
                        if u >= 1:
                            wd_item_sy(u, 0)
                            wd_item_sy(u, 1)
                        for e in range(E):
                            k = 8 * u + e
                            if j == 0 and e == 0:
                                sy.wait_ge(S["d_rww"], 16 * (c + 1))
                            if k >= 2:
                                pu, pe = divmod(k - 2, 8)
                                sy.wait_ge(S["v_h"], 40 * pu + 4 * (pe + 1))
                            ap = bass.AP(rwtd, ((c % 2) * NTC * E + 4 * j * E + e) * 128,
                                         [[0, 128], [E * 128, TPS], [1, 128]])
                            sy.dma_start(out=rwb[:, k % 2, :],
                                         in_=ap).then_inc(S["d_rwb" + str(k % 2)], 16)
                        if u >= 1:
                            for e in range(2, E):
                                wd_item_sy(u, e)
                        if j == 0 and c + 1 < NCH:
                            for i in range(NTC):
                                prepass_load(c + 1, i)
                            for i in range(3):
                                xln_reload(c + 1, i)
                        if j == 0:
                            for hs in range(1, 16, 2):
                                dg_load_sy(u_of(c, 1), hs)
                    if c + 1 < NCH:
                        for i in range(3, NTC):
                            xln_reload(c + 1, i)
                sy.wait_ge(S["d_st0"], 16 * 2 * 2 * NCH)
                sy.wait_ge(S["d_st1"], 16 * 2 * 2 * NCH)

            @blk.gpsimd
            def _(gp):
                gp.memset(xnT[:, :, 0:HALO], 0.0).then_inc(S["g_ms"], 1)
                gp.dma_start(out=wr_sb[:, :, :],
                             in_=wr_d.rearrange("(k p) f -> p k f", p=128)
                             ).then_inc(S["d_w0"], 16)

                PC = {"wg": (wg_d, wgbf, wg_sb, "d_wg", "d_pcg", "t_gate"),
                      "wu": (wu_d, wubf, wu_sb, "d_wu", "d_pcu", "t_up"),
                      "wd": (wd_d, wdbf, wd_sb, "d_wd", "d_pcd", "t_wdfree")}

                def ls(kind, e):
                    srcd, dstd, slot, csem, psem, gsem = PC[kind]
                    if e >= 2:
                        if kind == "wd":
                            gp.wait_ge(S["t_wdfree"], e - 1)
                        else:
                            wslot_wait(gp, gsem, e)
                        gp.wait_ge(S[psem + str(e % 2)], 16 * ((e - 2) // 2 + 1))
                    gp.dma_start(out=slot[:, e % 2],
                                 in_=srcd[e].rearrange("(k p) f -> p k f", p=128)
                                 ).then_inc(S[csem + str(e % 2)], 16)
                    gp.wait_ge(S[csem + str(e % 2)], slot2_cnt(e))
                    gp.dma_start(out=dstd[e].rearrange("(k p) f -> p k f", p=128),
                                 in_=slot[:, e % 2]).then_inc(S[psem + str(e % 2)], 16)

                def stream_item(kind, u, e):
                    srcd, dstd, slot, csem, psem, gsem = PC[kind]
                    m = 8 * u + e
                    wslot_wait(gp, gsem, m)
                    if m < 10:
                        gp.wait_ge(S[psem + str(m % 2)], 16 * 4)
                    gp.dma_start(out=slot[:, m % 2],
                                 in_=dstd[e].rearrange("(k p) f -> p k f", p=128)
                                 ).then_inc(S[csem + str(m % 2)], 16)

                def xres_st(u):
                    order = [("xres", 0), ("xres", 1), ("st", 0), ("xres", 2),
                             ("st", 1), ("xres", 3), ("st", 2), ("st", 3)]
                    for kind, q in order:
                        row = 128 * (u * TPS + q)
                        if kind == "xres":
                            if 4 * u + q >= 2:
                                gp.wait_ge(S["d_st" + str(q % 2)],
                                           xr_cnt(u, q) - 16)
                            gp.dma_start(out=outb[:, q % 2, :],
                                         in_=x_d[row:row + 128, :]
                                         ).then_inc(S["d_xr" + str(q % 2)], 16)
                        else:
                            gp.wait_ge(S["v_out"], 4 * u + q + 1)
                            gp.dma_start(out=out_d[row:row + 128, :],
                                         in_=outb[:, q % 2, :]
                                         ).then_inc(S["d_st" + str(q % 2)], 16)

                ls("wg", 0); ls("wu", 0); ls("wg", 1); ls("wu", 1)
                for e in range(2, E):
                    ls("wg", e)
                    ls("wu", e)
                    if e == 2:
                        gp.dma_start(out=sg_sb[:, :, :],
                                     in_=sg_d.rearrange("(k p) f -> p k f", p=128)
                                     ).then_inc(S["d_wsg"], 16)
                    if e == 3:
                        gp.dma_start(out=su_sb[:, :, :],
                                     in_=su_d.rearrange("(k p) f -> p k f", p=128)
                                     ).then_inc(S["d_wsu"], 16)
                    if e == 4:
                        gp.dma_start(out=sd_sb[:, :, :],
                                     in_=sd_d.rearrange("(k p) f -> p k f", p=128)
                                     ).then_inc(S["d_wsd"], 16)
                ls("wd", 0); ls("wd", 1)
                for e in range(2, E):
                    ls("wd", e)
                xres_st(0)
                for u in range(1, NSUB):
                    for e in range(E):
                        stream_item("wg", u, e)
                        stream_item("wu", u, e)
                    xres_st(u)

            @blk.vector
            def _(v):
                vn = [0]

                def ch(inst):
                    vn[0] += 1
                    inst.then_inc(S["vch"], 1)
                    v.wait_ge(S["vch"], vn[0])
                    return inst

                def mark(sem, n=1):
                    v.sem_inc(S[sem], n)

                def ln_prepass(cc):
                    for i in range(NTC):
                        v.wait_ge(S["d_xp" + str(i % 3)], xp_cnt(cc, i))
                        sl = xln[:, i % 3, :]
                        ch(v.bn_stats(stats[:, 0:6], sl[:, 0:512]))
                        ch(v.bn_stats(stats[:, 6:12], sl[:, 512:1024]))
                        ch(v.bn_aggr(aggr[:, 2 * i:2 * i + 2], stats[:, :]))
                        ch(v.tensor_scalar(varep[:, i:i + 1], aggr[:, 2 * i + 1:2 * i + 2],
                                           1e-5, None, OP.add))
                        mark("v_lnstat")

                ln_prepass(0)
                v.wait_ge(S["d_sq"], SQ_ALL)
                ch(v.tensor_mul(w2p[:, :], ct2[:, :], bmat2[:, :]))
                ch(v.tensor_scalar(w2n[:, :], w2p[:, :], -1.0, None, OP.mult))

                def ln_norm(cc, i):
                    g = g_of(cc, i)
                    v.wait_ge(S["a_rstd"], g + 1)
                    v.wait_ge(S["d_xl" + str(i % 3)], xl_cnt(cc, i))
                    if i >= 3:
                        v.wait_ge(S["t_tp"], 8 * (g - 2))
                    ch(v.tensor_scalar(xln[:, i % 3, :], xln[:, i % 3, :],
                                       aggr[:, 2 * i:2 * i + 1], rstd[:, i:i + 1],
                                       OP.subtract, OP.mult))
                    mark("v_ln")

                for c in range(NCH):
                    if c == 0:
                        for i in range(NTC):
                            ln_norm(c, i)

                    if c == 0:
                        for d in range(TAPS):
                            v.wait_ge(S["a_kexp"], d + 1)
                            sgn = w2p if d % 2 == 0 else w2n
                            ch(v.tensor_mul(pwb[:, :], pws[:, d % 2, :], sgn[:, :]))
                            mark("v_kb")
                            ch(v.reduce_sum(kdt[:, d, :],
                                            pwb[:, :].rearrange("p (a b) -> p a b", b=16),
                                            axis=AX.X))
                        ch(v.tensor_add(kdt[:, 0, :], kdt[:, 0, :], dpt_sb[:, :]))

                    def b3(t64):
                        return t64[:].rearrange("p (i e) -> p i e", e=E)

                    def bc8(t8):
                        return bass.AP(t8, 0, [[NTC, 128], [1, NTC], [0, E]])

                    v.wait_ge(S["t_rt"], NTC * (c + 1))
                    v.wait_ge(S["d_sq"], SQ_ALL)
                    ch(v.tensor_add(lg64[:, :], P[2][:, 0:NTC * E], eb_sb[:, :]))
                    ch(v.reduce_max(m1[:, :], b3(lg64), axis=AX.X))
                    ch(v.tensor_tensor(b3(eq1), b3(lg64), bc8(m1), OP.is_equal))
                    ch(v.tensor_scalar(msk[:, :], eq1[:, :], 1e9, None, OP.mult))
                    ch(v.tensor_sub(msk[:, :], lg64[:, :], msk[:, :]))
                    ch(v.reduce_max(m2[:, :], b3(msk), axis=AX.X))
                    ch(v.tensor_tensor(b3(eq2), b3(msk), bc8(m2), OP.is_equal))
                    ch(v.tensor_sub(e2v[:, :], m2[:, :], m1[:, :]))
                    mark("v_rtr1")
                    v.wait_ge(S["a_e2"], c + 1)
                    ch(v.tensor_scalar(den[:, :], e2x[:, :], 1.0, None, OP.add))
                    ch(v.reciprocal(w1v[:, :], den[:, :]))
                    ch(v.tensor_mul(w2v[:, :], e2x[:, :], w1v[:, :]))
                    ch(v.tensor_tensor(b3(rwf), b3(eq1), bc8(w1v), OP.mult))
                    ch(v.tensor_tensor(b3(eq2), b3(eq2), bc8(w2v), OP.mult))
                    ch(v.tensor_add(rwf[:, :], rwf[:, :], eq2[:, :]))
                    mark("v_rw")

                    if c == 0:
                        ib_ap = bass.AP(ib_sb, 0, [[128, 128], [0, HTAP], [1, 128]])
                        for k in range(16):
                            mb2, half2 = divmod(k, 2)
                            if k >= 1:
                                v.wait_ge(S["d_dgw"], 16 * k)
                            kap = bass.AP(kdt, (half2 * HTAP) * NMB + mb2,
                                          [[TAPS * NMB, 128], [NMB, HTAP], [0, 128]])
                            ch(v.tensor_tensor(dgG[:, :, :], ib_ap, kap, OP.mult))
                            mark("v_dg")

                    v.wait_ge(S["t_rwtp"], c + 1)
                    if c >= 1:
                        v.wait_ge(S["d_rww"], 16 * c)
                    ch(v.tensor_copy(rwTT[:, :], P[3][0:64, 0:128]))
                    mark("v_rwT")

                    for j in range(2):
                        u = u_of(c, j)
                        for uu in range(NUNIT):
                            for fb in range(NFB):
                                kk = 40 * u + 4 * uu + fb
                                if uu == 0 and fb == 0 and u >= 1:
                                    v.wait_ge(S["t_down"], 8 * u)
                                v.wait_ge(S["a_gev"], kk + 1)
                                if uu < E:
                                    if fb == 0:
                                        kk8 = 8 * u + uu
                                        v.wait_ge(S["d_rwb" + str(kk8 % 2)],
                                                  16 * (kk8 // 2 + 1))
                                    ch(v.tensor_mul(gev[:, kk % 2, :], gev[:, kk % 2, :],
                                                    rwb[:, (8 * u + uu) % 2, :]))
                                src0 = gev[:, kk % 2, :]
                                v.wait_ge(S["t_up"], kk + 1)
                                if uu < E:
                                    dst = h_sb[:, uu, fb, :]
                                else:
                                    dst = hsh[:, (uu - E) * 4 + fb, :]
                                ch(v.tensor_mul(dst, src0, P[fb][:, 0:512]))
                                mark("v_h")
                        if j == 1 and c + 1 < NCH:
                            v.wait_ge(S["t_dgfree"], 32 * c + 16)
                            v.wait_ge(S["v_evac"], 8 * c + 8)
                            ch(v.tensor_copy(xnT[:, :, 0:HALO], xnT[:, :, LC:LC + HALO]))
                            for i in range(3):
                                ln_norm(c + 1, i)
                        if j == 0 and c + 1 < NCH:
                            ln_prepass(c + 1)
                        for q in range(TPS):
                            v.wait_ge(S["a_g"], 8 * u + 2 * q + 2)
                            v.wait_ge(S["a_sT"], 8 * (u + 1))
                            ch(v.tensor_mul(ctmp[:, :], sT_sb[:, q, :],
                                            gcm2[:, q % 2, :]))
                            mark("v_c")
                            v.wait_ge(S["d_xr" + str(q % 2)], xr_cnt(u, q))
                            ch(v.tensor_add(outb[:, q % 2, :], outb[:, q % 2, :],
                                            ctmp[:, :]))
                            mark("v_out")
                            if j == 1 and c + 1 < NCH:
                                ln_norm(c + 1, 3 + q)
                        if j == 1 and c + 1 < NCH:
                            ln_norm(c + 1, 7)

            @blk.scalar
            def _(s):
                sn = [0]

                def ch(inst):
                    sn[0] += 1
                    inst.then_inc(S["sch"], 1)
                    s.wait_ge(S["sch"], sn[0])
                    return inst

                def mark(sem, n=1):
                    s.sem_inc(S[sem], n)

                s.dma_start(out=i_sb[:, :], in_=id_d[:, :]).then_inc(S["d_sq"], 16)
                s.dma_start(out=ib_sb[:, :], in_=idb_d[:, :]).then_inc(S["d_sq"], 16)
                s.dma_start(out=eb_sb[:, :], in_=eb_d[:, :]).then_inc(S["d_sq"], 16)
                s.dma_start(out=dpt_sb[:, :], in_=dpt_d[:, :]).then_inc(S["d_sq"], 16)
                s.dma_start(out=alog2[:, :], in_=alog_d[:, :]).then_inc(S["d_sq"], 16)
                s.dma_start(out=bmat2[:, :], in_=bmat_d[:, :]).then_inc(S["d_sq"], 16)
                s.dma_start(out=ct2[:, :], in_=ct_d[:, :]).then_inc(S["d_sq"], 16)
                def rstd_calc(cc):
                    for i in range(NTC):
                        g = g_of(cc, i)
                        s.wait_ge(S["v_lnstat"], g + 1)
                        ch(s.activation(tns[:, i:i + 1], varep[:, i:i + 1], AF.Ln))
                        ch(s.activation(rstd[:, i:i + 1], tns[:, i:i + 1],
                                        AF.Exp, scale=-0.5))
                        mark("a_rstd")

                def kexp(d):
                    s.wait_ge(S["d_sq"], SQ_ALL)
                    if d >= 2:
                        s.wait_ge(S["v_kb"], d - 1)
                    ch(s.activation(pws[:, d % 2, :], alog2[:, :], AF.Exp,
                                    scale=float(d)))
                    mark("a_kexp")

                rstd_calc(0)
                for d in range(2):
                    kexp(d)
                for c in range(NCH):
                    for i in range(NTC):
                        g = g_of(c, i)
                        s.wait_ge(S["t_tp"], 8 * (g + 1))
                        pp, pq = (P[4], P[5]) if i % 2 == 0 else (P[6], P[7])
                        col = HALO + i * 128
                        ch(s.activation(xnT[:, 0:4, col:col + 128],
                                        pp[:, 0:512].rearrange("p (a b) -> p a b", b=128),
                                        AF.Copy))
                        ch(s.activation(xnT[:, 4:8, col:col + 128],
                                        pq[:, 0:512].rearrange("p (a b) -> p a b", b=128),
                                        AF.Copy))
                        mark("v_evac")
                        if c == 0 and i == 3:
                            for d in range(2, 12):
                                kexp(d)
                        if c == 0 and i == 7:
                            for d in range(12, TAPS):
                                kexp(d)
                    s.wait_ge(S["v_rtr1"], c + 1)
                    ch(s.activation(e2x[:, :], e2v[:, :], AF.Exp))
                    mark("a_e2")
                    for j in range(2):
                        u = u_of(c, j)
                        if j == 1 and c + 1 < NCH:
                            rstd_calc(c + 1)

                        def dg_load(hs):
                            m = 16 * u + hs
                            if u == 0:
                                s.wait_ge(S["d_dgw"], 16 * (hs + 1))
                            if m >= 4:
                                s.wait_ge(S["t_dgfree"], m - 3)
                            mb2, half2 = divmod(hs, 2)
                            s.dma_start(out=dg_sb[:, m % 4], in_=dgdram[mb2, half2]
                                        ).then_inc(S["d_dg" + str(m % 4)], 16)

                        def dgw(k):
                            s.wait_ge(S["v_dg"], k + 1)
                            mb2, half2 = divmod(k, 2)
                            s.dma_start(out=dgdram[mb2, half2],
                                        in_=dgG[:, :, :]).then_inc(S["d_dgw"], 16)

                        if u == 0:
                            dgw(0)
                            dgw(1)
                            dg_load(0)
                            dgw(2)
                            dgw(3)
                            dg_load(2)
                        for mb in range(NMB):
                            if u == 0 and mb < 6:
                                dgw(2 * mb + 4)
                                dgw(2 * mb + 5)
                            if mb < 6:
                                dg_load(2 * mb + 4)
                            s.wait_ge(S["t_dgfree"], 16 * u + 2 * mb + 2)
                            if mb >= 2:
                                s.wait_ge(S["t_sT"], 8 * u + mb - 1)
                            ch(s.activation(stg[:, mb % 2, :], P[mb % 2][:, 0:512], AF.Copy))
                            mark("a_s")
                            if mb >= 1:
                                s.wait_ge(S["t_sT"], 8 * u + mb)
                                ch(s.activation(
                                    sT_sb[:, 0:TPS, (mb - 1) * 128:mb * 128],
                                    P[2 + (mb - 1) % 2][:, 0:512]
                                    .rearrange("p (a b) -> p a b", b=128),
                                    AF.Copy))
                                mark("a_sT")
                        s.wait_ge(S["t_sT"], 8 * u + 8)
                        ch(s.activation(
                            sT_sb[:, 0:TPS, (NMB - 1) * 128:NMB * 128],
                            P[2 + (NMB - 1) % 2][:, 0:512]
                            .rearrange("p (a b) -> p a b", b=128),
                            AF.Copy))
                        mark("a_sT")
                        for uu in range(NUNIT):
                            for fb in range(NFB):
                                kk = 40 * u + 4 * uu + fb
                                s.wait_ge(S["t_gate"], kk + 1)
                                if kk >= 2:
                                    s.wait_ge(S["v_h"], kk - 1)
                                ch(s.activation(gev[:, kk % 2, :], P[4 + fb][:, 0:512], AF.Silu))
                                mark("a_gev")
                        for b in range(NMB):
                            q, dh = divmod(b, 2)
                            kk = 8 * u + b
                            s.wait_ge(S["t_down"], kk + 1)
                            if q >= 2:
                                s.wait_ge(S["v_c"], 4 * u + q - 1)
                            ch(s.activation(gcm2[:, q % 2, dh * 512:(dh + 1) * 512],
                                            P[b][:, 0:512], AF.Silu))
                            mark("a_g")
                        if u + 1 < NSUB:
                            for hs2 in (0, 2):
                                m2 = 16 * (u + 1) + hs2
                                s.wait_ge(S["t_dgfree"], m2 - 3)
                                mb3, half3 = divmod(hs2, 2)
                                s.dma_start(out=dg_sb[:, m2 % 4], in_=dgdram[mb3, half3]
                                            ).then_inc(S["d_dg" + str(m2 % 4)], 16)

            @blk.tensor
            def _(t):
                def sT_tp(u, j, c, mb):
                    t.wait_ge(S["a_s"], 8 * u + mb + 1)
                    if mb >= 2:
                        t.wait_ge(S["a_sT"], 8 * u + mb - 1)
                    else:
                        if u >= 1:
                            t.wait_ge(S["a_sT"], 8 * (u - 1) + 7 + mb)
                        if mb == 0:
                            t.wait_ge(S["v_rtr1"], c + 1)
                        if mb == 1 and j == 1:
                            t.wait_ge(S["v_rwT"], c + 1)
                    ti = None
                    for q in range(TPS):
                        ti = t.transpose(P[2 + mb % 2][:, q * 128:(q + 1) * 128],
                                         stg[:, mb % 2, q * 128:(q + 1) * 128],
                                         i_sb[:, :])
                    ti.then_inc(S["t_sT"], 1)

                def ln_pair(i):
                    return (P[4], P[5]) if i % 2 == 0 else (P[6], P[7])

                def ln_tp(cc, i, bank_wait=True):
                    g = g_of(cc, i)
                    t.wait_ge(S["v_ln"], g + 1)
                    if bank_wait and i >= 2:
                        t.wait_ge(S["v_evac"], g - 1)
                    pp, pq = ln_pair(i)
                    for q in range(NMB):
                        dst = (pp if q < 4 else pq)[:, (q % 4) * 128:(q % 4 + 1) * 128]
                        t.transpose(dst, xln[:, i % 3, q * 128:(q + 1) * 128],
                                    i_sb[:, :]).then_inc(S["t_tp"], 1)

                for c in range(NCH):
                    for i in range(0 if c == 0 else 2, NTC):
                        ln_tp(c, i)
                    for i in range(NTC):
                        g = g_of(c, i)
                        t.wait_ge(S["v_evac"], g + 1)
                        if i == 0:
                            if c > 0:
                                t.wait_ge(S["a_sT"], 16 * c)
                                t.wait_ge(S["v_rtr1"], c)
                                t.wait_ge(S["v_rwT"], c)
                            else:
                                t.wait_ge(S["d_w0"], 16)
                        mi = None
                        for k in range(NKS):
                            mi = t.matmul(P[2][:, E * i:E * (i + 1)],
                                          xnT[:, k, HALO + i * 128:HALO + (i + 1) * 128],
                                          wr_sb[:, k, :], start=(k == 0), stop=(k == NKS - 1))
                        mi.then_inc(S["t_rt"], 1)
                    for j in range(2):
                        u = u_of(c, j)
                        t.wait_ge(S["v_evac"], NTC * c + TPS * (j + 1))
                        if u == 0:
                            t.wait_ge(S["g_ms"], 1)
                        else:
                            t.wait_ge(S["a_g"], 8 * (u - 1) + 2)
                        for mb in range(NMB):
                            mi = None
                            for half in range(2):
                                m = 16 * u + 2 * mb + half
                                t.wait_ge(S["d_dg" + str(m % 4)], 16 * (m // 4 + 1))
                                if mb >= 2 and half == 0:
                                    t.wait_ge(S["a_s"], 8 * u + mb - 1)
                                for dd in range(HTAP):
                                    d = half * HTAP + dd
                                    rhs = xnT[:, mb,
                                              HALO + j * SUB - d:HALO + j * SUB - d + SUB]
                                    mi = t.matmul(P[mb % 2][:, 0:512], dg_sb[:, m % 4, dd, :],
                                                  rhs, start=(d == 0), stop=(d == TAPS - 1))
                                mi.then_inc(S["t_dgfree"], 1)
                            if mb >= 1:
                                sT_tp(u, j, c, mb - 1)
                        sT_tp(u, j, c, NMB - 1)
                        if j == 0:
                            t.wait_ge(S["v_rw"], c + 1)
                            t.wait_ge(S["a_sT"], 8 * u + 8)
                            t.transpose(P[3][0:64, 0:128], rwf[:, :],
                                        i_sb[:, :]).then_inc(S["t_rwtp"], 1)
                        if u >= 1:
                            t.wait_ge(S["a_g"], 8 * u)
                        for uu in range(NUNIT):
                            m = 8 * u + uu
                            if uu < E:
                                t.wait_ge(S["d_wg" + str(m % 2)], slot2_cnt(m))
                            elif uu == E and u == 0:
                                t.wait_ge(S["d_wsg"], 16)
                            for fb in range(NFB):
                                kk = 40 * u + 4 * uu + fb
                                if kk >= 4:
                                    t.wait_ge(S["a_gev"], kk - 3)
                                else:
                                    t.wait_ge(S["v_evac"], NTC * c + (7 if fb < 2 else 8))
                                mi = None
                                for k in range(NKS):
                                    if uu < E:
                                        lhs = wg_sb[:, m % 2, k, fb * 128:(fb + 1) * 128]
                                    else:
                                        fsb = (uu - E) * 4 + fb
                                        lhs = sg_sb[:, k, fsb * 128:(fsb + 1) * 128]
                                    rhs = xnT[:, k, HALO + j * SUB:HALO + (j + 1) * SUB]
                                    mi = t.matmul(P[4 + fb][:, 0:512], lhs, rhs,
                                                  start=(k == 0), stop=(k == NKS - 1))
                                mi.then_inc(S["t_gate"], 1)
                            if uu < E:
                                t.wait_ge(S["d_wu" + str(m % 2)], slot2_cnt(m))
                            elif uu == E and u == 0:
                                t.wait_ge(S["d_wsu"], 16)
                            for fb in range(NFB):
                                kk = 40 * u + 4 * uu + fb
                                if kk >= 4:
                                    t.wait_ge(S["v_h"], kk - 3)
                                if uu == 0:
                                    if fb < 2:
                                        t.wait_ge(S["a_s"], 8 * u + 8)
                                    else:
                                        t.wait_ge(S["a_sT"], 8 * u + 5 + fb)
                                        t.wait_ge(S["v_rtr1"], c + 1)
                                        t.wait_ge(S["v_rwT"], c + 1)
                                mi = None
                                for k in range(NKS):
                                    if uu < E:
                                        lhs = wu_sb[:, m % 2, k, fb * 128:(fb + 1) * 128]
                                    else:
                                        fsb = (uu - E) * 4 + fb
                                        lhs = su_sb[:, k, fsb * 128:(fsb + 1) * 128]
                                    rhs = xnT[:, k, HALO + j * SUB:HALO + (j + 1) * SUB]
                                    mi = t.matmul(P[fb][:, 0:512], lhs, rhs,
                                                  start=(k == 0), stop=(k == NKS - 1))
                                mi.then_inc(S["t_up"], 1)

                        t.wait_ge(S["v_h"], 40 * (u + 1))
                        t.wait_ge(S["a_gev"], 40 * (u + 1))
                        for uu in range(NUNIT):
                            if uu < E:
                                m = 8 * u + uu
                                t.wait_ge(S["d_wd" + str(m % 2)], slot2_cnt(m))
                            elif uu == E and u == 0:
                                t.wait_ge(S["d_wsd"], 16)
                            for fs in range(NFB):
                                for q in range(TPS):
                                    if uu < E:
                                        lhs = h_sb[:, uu, fs, q * 128:(q + 1) * 128]
                                    else:
                                        fsb = (uu - E) * 4 + fs
                                        lhs = hsh[:, fsb, q * 128:(q + 1) * 128]
                                    for dh in range(2):
                                        if uu < E:
                                            rhs = wd_sb[:, m % 2, fs,
                                                        dh * 512:(dh + 1) * 512]
                                        else:
                                            fsb = (uu - E) * 4 + fs
                                            rhs = sd_sb[:, fsb, dh * 512:(dh + 1) * 512]
                                        mi = t.matmul(P[2 * q + dh][:, 0:512], lhs, rhs,
                                                      start=(uu == 0 and fs == 0),
                                                      stop=(uu == NUNIT - 1 and fs == NFB - 1))
                                        if uu == NUNIT - 1 and fs == NFB - 1:
                                            mi.then_inc(S["t_down"], 1)
                                        if (uu < E and fs == NFB - 1 and q == TPS - 1
                                                and dh == 1):
                                            mi.then_inc(S["t_wdfree"], 1)
                        if j == 1 and c + 1 < NCH:
                            t.wait_ge(S["a_g"], 8 * u + 6)
                            ln_tp(c + 1, 0, bank_wait=False)
                            t.wait_ge(S["a_g"], 8 * u + 8)
                            ln_tp(c + 1, 1, bank_wait=False)
    return nc


def _host_layout(inputs, T_full):
    f32 = np.float32

    def smix(a):
        return np.ascontiguousarray(
            np.asarray(a, dtype=f32).reshape(16, 8, 128).transpose(2, 1, 0)
            .reshape(128, 128))

    x = np.ascontiguousarray(inputs["L0_attn_x"], dtype=f32)
    dpt = np.ascontiguousarray(np.asarray(inputs["L0_attn_D"], dtype=f32)
                               .reshape(NMB, 128).T)
    eb = np.ascontiguousarray(
        np.broadcast_to(np.tile(np.asarray(inputs["L0_ffn_expert_bias"], dtype=f32), 8)
                        .reshape(1, 8 * E), (128, 8 * E)))
    common = {
        "alog": smix(inputs["L0_attn_A_log"]),
        "bmat": smix(inputs["L0_attn_B"]),
        "ct": smix(np.asarray(inputs["L0_attn_C"], dtype=f32).T),
        "dpt": dpt,
        "wr": np.ascontiguousarray(inputs["L0_ffn_Wr"], dtype=f32),
        "wg": np.ascontiguousarray(inputs["L0_ffn_Wg"], dtype=f32),
        "wu": np.ascontiguousarray(inputs["L0_ffn_Wu"], dtype=f32),
        "wd": np.ascontiguousarray(inputs["L0_ffn_Wd"], dtype=f32),
        "sgm": np.ascontiguousarray(inputs["L0_ffn_Sg"], dtype=f32),
        "sux": np.ascontiguousarray(inputs["L0_ffn_Su"], dtype=f32),
        "sdm": np.ascontiguousarray(inputs["L0_ffn_Sd"], dtype=f32),
        "ebias": eb,
        "ident": np.eye(128, dtype=f32),
        "identb": np.eye(128, dtype=ml_dtypes.bfloat16),
    }
    return x, common


def kernel(L0_attn_x, L0_attn_A_log, L0_attn_B, L0_attn_C, L0_attn_D,
           L0_ffn_Wr, L0_ffn_Wg, L0_ffn_Wu, L0_ffn_Wd,
           L0_ffn_Sg, L0_ffn_Su, L0_ffn_Sd, L0_ffn_expert_bias):
    inputs = dict(L0_attn_x=L0_attn_x, L0_attn_A_log=L0_attn_A_log,
                  L0_attn_B=L0_attn_B, L0_attn_C=L0_attn_C, L0_attn_D=L0_attn_D,
                  L0_ffn_Wr=L0_ffn_Wr, L0_ffn_Wg=L0_ffn_Wg, L0_ffn_Wu=L0_ffn_Wu,
                  L0_ffn_Wd=L0_ffn_Wd, L0_ffn_Sg=L0_ffn_Sg, L0_ffn_Su=L0_ffn_Su,
                  L0_ffn_Sd=L0_ffn_Sd, L0_ffn_expert_bias=L0_ffn_expert_bias)
    B, T, D = L0_attn_x.shape
    assert B == 8 and D == D_MODEL
    x, common = _host_layout(inputs, T)
    nc = build(T)
    in_maps = [dict(common, x=np.ascontiguousarray(x[b])) for b in range(B)]
    res = run_bass_kernel_spmd(nc, in_maps, core_ids=list(range(B)))
    out = np.stack([res.results[b]["out"] for b in range(B)], axis=0)
    return out.astype(np.float32)
